# version 01 (final)
# baseline (speedup 1.0000x reference)
import math
from contextlib import ExitStack

import numpy as np
import ml_dtypes

import concourse.bass as bass
import concourse.mybir as mybir
from concourse.bass_utils import run_bass_kernel_spmd

F32 = mybir.dt.float32
BF16 = mybir.dt.bfloat16
AF = mybir.ActivationFunctionType
ALU = mybir.AluOpType
AX = mybir.AxisListType

MAGIC = 1.5 * 2.0 ** 23
PI = math.pi
EPS = 1e-5
NT = 16
NHC = 32
ALVL = 9
AVAR = ''

ENGINES = ("sync", "gpsimd", "tensor", "vector", "scalar")

SEMS = ["sT", "sV", "sS", "sG", "sCC", "sCCD",
        "sW0", "sW1", "sW2",
        "sQa0", "sQa1", "sQb0", "sQb1", "sQo0", "sQo1",
        "sH0", "sH1", "sY0", "sY1", "sQc", "sGc"]


class Sched:
    def __init__(self):
        self.ops = {e: [] for e in ENGINES}
        self.counts = {s: 0 for s in SEMS}

    def emit(self, eng, fn, inc=None, after=()):
        toks = []
        for t in after:
            if t is None:
                continue
            if isinstance(t, list):
                toks.extend(x for x in t if x is not None)
            else:
                toks.append(t)
        tok = None
        if inc is not None:
            s, amt = inc
            self.counts[s] += amt
            tok = (s, self.counts[s])
        self.ops[eng].append((toks, fn, inc))
        return tok


def build(debug=False, stop='F'):
    nc = bass.Bass("TRN2", target_bir_lowering=False)

    def din(name, shape, dt=F32):
        return nc.dram_tensor(name, shape, dt, kind="ExternalInput").ap()

    x_in = din("x_in", [NT, 128, 1024])
    wq_in = din("wq_in", [128, 8, 1024]); wk_in = din("wk_in", [128, 8, 1024])
    wv_in = din("wv_in", [128, 8, 1024]); wo_in = din("wo_in", [128, 8, 1024])
    w1_in = din("w1_in", [128, 8, 4096]); w2_in = din("w2_in", [128, 32, 1024])
    mw1_in = din("mw1_in", [8, 128, 8, 512]); mw3_in = din("mw3_in", [8, 128, 8, 512])
    mw2_in = din("mw2_in", [8, 128, 4, 1024])
    proj_in = din("proj_in", [64, 128])
    ln1g_in = din("ln1g_in", [128, 8]); ln1b_in = din("ln1b_in", [128, 8])
    ln2g_in = din("ln2g_in", [128, 8]); ln2b_in = din("ln2b_in", [128, 8])
    bqf_in = din("bqf_in", [128, 8]); bkf_in = din("bkf_in", [128, 8])
    b1f_in = din("b1f_in", [128, 32])
    bv_in = din("bv_in", [1, 1024]); bo_in = din("bo_in", [1, 1024]); b2_in = din("b2_in", [1, 1024])
    gate_in = din("gate_in", [128, 64])
    ident_in = nc.dram_tensor("ident_in", [128, 128], BF16, kind="ExternalInput").ap()
    sel_in = nc.dram_tensor("sel_in", [128, 1024], BF16, kind="ExternalInput").ap()
    cst_in = din("cst_in", [128, 8])

    out_hbm = nc.dram_tensor("out_hbm", [NT, 128, 1024], F32, kind="ExternalOutput").ap()

    dbg = dict(kind="ExternalOutput") if debug else {}
    a_hbm = nc.dram_tensor("a_hbm", [NT, 128, 1024], F32, **dbg).ap()
    h_hbm = nc.dram_tensor("h_hbm", [NT, 128, 4096], BF16, **dbg).ap()
    x4_hbm = nc.dram_tensor("x4_hbm", [NT, 128, 1024], F32, **dbg).ap()
    y_hbm = nc.dram_tensor("y_hbm", [NT, 128, 1024], F32, **dbg).ap()
    ccin = nc.dram_tensor("ccin", [128, 2048], F32).ap()
    ccout = nc.dram_tensor("ccout", [128, 2048], F32).ap()
    if debug:
        kv_dbg = nc.dram_tensor("kv_dbg", [128, 2048], F32, kind="ExternalOutput").ap()
        x2_dbg = nc.dram_tensor("x2_dbg", [128, 8, 2048], BF16, kind="ExternalOutput").ap()
        scl_dbg = nc.dram_tensor("scl_dbg", [128, 128], F32, kind="ExternalOutput").ap()

    S = Sched()

    with ExitStack() as stack:
        def sbuf(name, shape, dt):
            return stack.enter_context(nc.sbuf_tensor(name, shape, dt))

        FM = sbuf("FM", [128, 8, 2048], BF16)
        WB = sbuf("WB", [128, 45056], BF16)
        SA = sbuf("SA", [128, 2, 1024], F32)
        SBf = sbuf("SBf", [128, 2, 1024], F32)
        SO = sbuf("SO", [128, 2, 1024], F32)
        HB = sbuf("HB", [128, 8192], BF16)
        SC1 = sbuf("SC1", [128, 4096], BF16)
        SC2 = sbuf("SC2", [128, 4096], BF16)
        SC3 = sbuf("SC3", [128, 1024], BF16)
        X2N = sbuf("X2N", [128, 1024], BF16)
        KV = sbuf("KV", [128, 2048], F32)
        KVB = sbuf("KVB", [128, 2048], BF16)
        FS1 = sbuf("FS1", [128, 1024], F32)
        FS2 = sbuf("FS2", [128, 1024], F32)
        STT = sbuf("STT", [128, 64], F32)
        GWB = sbuf("GWB", [128, 64], BF16)
        LN1G = sbuf("LN1G", [128, 8], F32); LN1B = sbuf("LN1B", [128, 8], F32)
        LN2G = sbuf("LN2G", [128, 8], F32); LN2B = sbuf("LN2B", [128, 8], F32)
        BQF = sbuf("BQF", [128, 8], F32); BKF = sbuf("BKF", [128, 8], F32)
        B1F = sbuf("B1F", [128, 32], F32)
        BROW = sbuf("BROW", [8, 1024], BF16)
        IDN = sbuf("IDN", [128, 128], BF16)
        PROJD = sbuf("PROJD", [128, 256], BF16)
        SEL = sbuf("SEL", [128, 1024], BF16)
        SCL = sbuf("SCL", [128, 128], F32)
        SCLB = sbuf("SCLB", [128, 8], BF16)
        CST = sbuf("CST", [128, 8], F32)

        PP = [stack.enter_context(nc.psum_tensor(f"PP{i}", [128, 512], F32)) for i in range(6)]
        PT = stack.enter_context(nc.psum_tensor("PT", [128, 1024], BF16))

        sems = {s: stack.enter_context(nc.semaphore(s)) for s in SEMS}

        def t_op(fn, after=()):
            return S.emit("tensor", fn, ("sT", 1), after)

        def v_op(fn, after=()):
            return S.emit("vector", fn, ("sV", 1), after)

        def s_op(fn, after=()):
            return S.emit("scalar", fn, ("sS", 1), after)

        def g_op(fn, after=()):
            return S.emit("gpsimd", fn, ("sG", 1), after)

        def q_dma(fn, sem, after=()):
            return S.emit("sync", fn, (sem, 16), after)

        def g_dma(fn, sem, after=()):
            return S.emit("gpsimd", fn, (sem, 16), after)

        WK0, WV0, WQ0, WO0 = 0, 8192, 16384, 24576

        def w1col(kc):
            if kc < 3:
                return 32768 + kc * 4096
            if kc < 7:
                return (kc - 3) * 4096
            return 16384

        def w2col(hc):
            if hc < 12:
                return 20480 + hc * 1024
            if hc < 28:
                return (hc - 12) * 1024
            return 16384 + (hc - 28) * 1024

        EBASE = {0: 32768, 1: 0, 2: 12288, 3: 24576, 4: 0, 5: 12288, 6: 24576, 7: 0}

        def ecol(e, piece, off):
            return EBASE[e] + piece * 4096 + off

        pp_free = [None] * 6
        pp_wr = [None] * 6
        pt_free = None
        fs_war = []
        stt_war = []
        sc1_war = []
        sc2_war = []
        sc3_war = []
        fm_war = []
        sout_free = [None, None]

        cl = []
        for dst, src in [(LN1G, ln1g_in), (LN1B, ln1b_in), (LN2G, ln2g_in),
                         (LN2B, ln2b_in), (BQF, bqf_in), (BKF, bkf_in),
                         (B1F, b1f_in), (CST, cst_in)]:
            cl.append(q_dma(lambda q, d=dst, s=src: q.dma_start(out=d[:, :], in_=s), "sQc"))
        cl.append(q_dma(lambda q: q.dma_start(out=IDN[:, :], in_=ident_in), "sQc"))
        cl.append(q_dma(lambda q: q.dma_start(out=SEL[:, :], in_=sel_in), "sQc"))
        tok_consts = cl[-1]

        tok_bz = v_op(lambda v: v.memset(BROW[:, :], 0.0))
        gc = []
        gc.append(g_dma(lambda g: g.dma_start(out=BROW[0:1, :], in_=bv_in), "sGc", after=[tok_bz]))
        gc.append(g_dma(lambda g: g.dma_start(out=BROW[1:2, :], in_=bo_in), "sGc"))
        gc.append(g_dma(lambda g: g.dma_start(out=BROW[2:3, :], in_=b2_in), "sGc"))
        gc.append(g_dma(lambda g: g.dma_start(out=GWB[:, :], in_=gate_in), "sGc"))
        tok_pz = v_op(lambda v: v.memset(PROJD[:, :], 0.0))
        gc.append(g_dma(lambda g: g.dma_start(out=PROJD[0:64, 0:128], in_=proj_in), "sGc",
                        after=[tok_pz]))
        gc.append(g_dma(lambda g: g.dma_start(out=PROJD[64:128, 128:256], in_=proj_in), "sGc"))
        tok_gc = gc[-1]
        tok_proj = v_op(lambda v: v.tensor_scalar_mul(PROJD[:, :], PROJD[:, :],
                                                      1.0 / (16.0 * PI)), after=[tok_gc, tok_pz])

        wA = [g_dma(lambda g: g.dma_start(
            out=WB[:, WK0:WK0 + 8192].rearrange("p (c f) -> p c f", f=1024), in_=wk_in), "sW0"),
            g_dma(lambda g: g.dma_start(
                out=WB[:, WV0:WV0 + 8192].rearrange("p (c f) -> p c f", f=1024), in_=wv_in), "sW0")]
        tok_wA = wA[-1]
        wBb = [g_dma(lambda g: g.dma_start(
            out=WB[:, WQ0:WQ0 + 8192].rearrange("p (c f) -> p c f", f=1024), in_=wq_in), "sW1"),
            g_dma(lambda g: g.dma_start(
                out=WB[:, WO0:WO0 + 8192].rearrange("p (c f) -> p c f", f=1024), in_=wo_in), "sW1")]
        tok_wB = wBb[-1]
        w1_toks = {}
        for kc in range(3):
            w1_toks[kc] = g_dma(lambda g, kc=kc: g.dma_start(
                out=WB[:, w1col(kc):w1col(kc) + 4096], in_=w1_in[:, kc, :]), "sW2")

        tok_xload = [None] * NT
        tok_xload[0] = q_dma(lambda q: q.dma_start(out=SA[:, 0, :], in_=x_in[0]), "sQa0")
        tok_xload[1] = q_dma(lambda q: q.dma_start(out=SA[:, 1, :], in_=x_in[1]), "sQa1")

        def ln_stats(src_fn, after):
            nonlocal stt_war
            t1 = v_op(lambda v: v.bn_stats(STT[:, 0:6], src_fn()[:, 0:512]),
                      after=list(after) + stt_war)
            t2 = v_op(lambda v: v.bn_stats(STT[:, 6:12], src_fn()[:, 512:1024]),
                      after=list(after))
            t3 = v_op(lambda v: v.bn_aggr(STT[:, 12:14], STT[:, 0:12]), after=[t1, t2])
            t4 = v_op(lambda v: v.tensor_scalar_add(STT[:, 14:15], STT[:, 13:14], EPS),
                      after=[t3])
            t5 = s_op(lambda s: s.sqrt(STT[:, 15:16], STT[:, 14:15]), after=[t4])
            t6 = v_op(lambda v: v.reciprocal(STT[:, 16:17], STT[:, 15:16]), after=[t5])
            t7 = v_op(lambda v: v.scalar_tensor_tensor(
                STT[:, 17:18], STT[:, 12:13], -1.0, STT[:, 16:17], ALU.mult, ALU.mult),
                after=[t6])
            return t7

        def fourier_pack(pp, h0, after_psum):
            nonlocal fs_war, sc2_war
            t1 = v_op(lambda v: v.tensor_scalar_add(FS1[:, 0:512], pp[:, 0:512], MAGIC),
                      after=list(after_psum) + fs_war)
            t2 = v_op(lambda v: v.tensor_scalar(FS1[:, 512:1024], FS1[:, 0:512], MAGIC,
                                                None, ALU.subtract), after=[t1])
            t3 = v_op(lambda v: v.tensor_tensor(FS2[:, 0:512], pp[:, 0:512],
                                                FS1[:, 512:1024], ALU.subtract),
                      after=[t2])
            t4 = v_op(lambda v: v.scalar_tensor_tensor(FS2[:, 512:1024], FS2[:, 0:512],
                                                       -1.0, FS2[:, 0:512],
                                                       ALU.mult, ALU.max), after=[t3])
            sin_out = SC2[:, (2 * h0 + 1) * 128:(2 * h0 + 8) * 128].rearrange(
                "p (h f) -> p h f", f=128)[:, ::2, :]
            cos_out = SC2[:, (2 * h0) * 128:(2 * h0 + 8) * 128].rearrange(
                "p (h f) -> p h f", f=128)[:, ::2, :]
            rin = FS2[:, 0:512].rearrange("p (h f) -> p h f", f=128)
            ain = FS2[:, 512:1024].rearrange("p (h f) -> p h f", f=128)
            t5 = s_op(lambda s: s.activation(sin_out, rin, AF.Sin, scale=2 * PI),
                      after=[t3, t4, tok_consts] + sc2_war)
            t6 = s_op(lambda s: s.activation(cos_out, ain, AF.Sin, scale=-2 * PI,
                                             bias=CST[:, 0:1]), after=[t4, t5])
            fs_war = [t2, t3, t4, t5, t6]
            return t3, [t5, t6]

        tok_x2fm = [None] * NT
        tok_x2n_done = [None] * NT
        tok_kv_add = None
        tok_kfm_last = None

        for g in range(4):
            tiles = [4 * g + j for j in range(4)]
            for j, t in enumerate(tiles):
                sal = SA[:, t % 2, :]
                tok_st = ln_stats(lambda sal=sal: sal, [tok_xload[t], tok_consts])
                tok_x2n = s_op(lambda s, sal=sal: s.activation(
                    X2N[:, :], sal, AF.Identity, bias=STT[:, 17:18], scale=STT[:, 16:17]),
                    after=[tok_st, pt_free])
                stt_war = [tok_x2n]
                tok_x2n_done[t] = tok_x2n
                if t + 2 < NT:
                    tok_xload[t + 2] = q_dma(
                        lambda q, t2=t + 2: q.dma_start(out=SA[:, t2 % 2, :], in_=x_in[t2]),
                        "sQa0" if t % 2 == 0 else "sQa1", after=[tok_x2n])

                def tp(tensor):
                    last = None
                    for c in range(8):
                        last = tensor.transpose(PT[:, c * 128:(c + 1) * 128],
                                                X2N[:, c * 128:(c + 1) * 128], IDN[:, :])
                    return last
                tok_T = t_op(tp, after=[tok_x2n, pt_free, tok_consts])
                ev = []
                for c in range(8):
                    ev.append(v_op(lambda v, c=c, t=t: v.tensor_scalar(
                        FM[:, c, t * 128:(t + 1) * 128], PT[:, c * 128:(c + 1) * 128],
                        LN1G[:, c:c + 1], LN1B[:, c:c + 1], ALU.mult, ALU.add),
                        after=[tok_T] + fm_war))
                pt_free = ev[-1]
                tok_x2fm[t] = ev[-1]
            fm_war = []

            aft_fm = [tok_x2fm[t] for t in tiles]
            if ALVL < 1:
                continue
            kfm_ev = []
            for co in range(8):
                bank = co % 2
                def kmm(tensor, co=co, bank=bank, g=g):
                    last = None
                    for kc in range(8):
                        last = tensor.matmul(
                            PP[bank][:, 0:512],
                            WB[:, WK0 + kc * 1024 + co * 128:WK0 + kc * 1024 + co * 128 + 128],
                            FM[:, kc, g * 512:(g + 1) * 512],
                            start=(kc == 0), stop=(kc == 7))
                    return last
                tok_k = t_op(kmm, after=aft_fm + [tok_wA, pp_free[bank]])
                pp_wr[bank] = tok_k
                tok_e = s_op(lambda s, co=co, bank=bank: s.activation(
                    SC1[:, co * 512:(co + 1) * 512], PP[bank][:, 0:512], AF.Identity,
                    bias=BKF[:, co:co + 1]), after=[tok_k, tok_consts] + sc1_war)
                pp_free[bank] = tok_e
                kfm_ev.append(tok_e)
            sc1_war = []
            tok_kfm_last = kfm_ev[-1]

            for j, t in enumerate(tiles):
                if ALVL < 2:
                    continue
                kp_toks = []
                for pk in range(4):
                    bank = 2 + pk % 2
                    def pmm(tensor, pk=pk, bank=bank, j=j):
                        last = None
                        for pr in range(2):
                            c = 2 * pk + pr
                            last = tensor.matmul(
                                PP[bank][:, pr * 256:(pr + 1) * 256],
                                SC1[:, c * 512 + j * 128:c * 512 + j * 128 + 128],
                                PROJD[:, :],
                                start=True, stop=True)
                        return last
                    tok_p = t_op(pmm, after=[tok_kfm_last, tok_proj, pp_free[bank]])
                    pp_wr[bank] = tok_p
                    tok_pfree, outs = fourier_pack(PP[bank], 4 * pk, [tok_p])
                    pp_free[bank] = tok_pfree
                    kp_toks.extend(outs)
                sc2_war = []

                if ALVL < 3:
                    continue
                v_ev = []
                for half in range(2):
                    bank = 4 + half
                    def vmm(tensor, half=half, bank=bank, t=t):
                        for kc in range(8):
                            tensor.matmul(
                                PP[bank][:, 0:512],
                                FM[:, kc, t * 128:(t + 1) * 128],
                                WB[:, WV0 + kc * 1024 + half * 512:
                                   WV0 + kc * 1024 + half * 512 + 512],
                                start=(kc == 0), stop=False)
                        return tensor.matmul(
                            PP[bank][:, 0:512], SEL[0:8, 0:128],
                            BROW[0:8, half * 512:half * 512 + 512],
                            start=False, stop=True)
                    tok_v = t_op(vmm, after=[tok_x2fm[t], tok_wA, pp_free[bank], tok_gc])
                    pp_wr[bank] = tok_v
                    tok_ve = v_op(lambda v, half=half, bank=bank: v.tensor_copy(
                        SC3[:, half * 512:half * 512 + 512], PP[bank][:, 0:512]),
                        after=[tok_v] + sc3_war)
                    pp_free[bank] = tok_ve
                    v_ev.append(tok_ve)
                sc3_war = []

                if ALVL < 4:
                    continue
                kv_mm_toks = []
                for bank in range(4):
                    def kvmm(tensor, bank=bank):
                        last = None
                        for k in range(8):
                            idx = bank * 8 + k
                            h, half = idx // 2, idx % 2
                            last = tensor.matmul(
                                PP[bank][:, k * 64:(k + 1) * 64],
                                SC2[:, idx * 128:(idx + 1) * 128],
                                SC3[:, h * 64:(h + 1) * 64],
                                start=True, stop=True)
                        return last
                    tok_kv = t_op(kvmm, after=kp_toks + v_ev + [pp_free[bank]])
                    pp_wr[bank] = tok_kv
                    kv_mm_toks.append((bank, tok_kv))
                for bank, tok_kv in kv_mm_toks:
                    if t == 0:
                        tok_a = v_op(lambda v, bank=bank: v.tensor_copy(
                            KV[:, bank * 512:(bank + 1) * 512], PP[bank][:, 0:512]),
                            after=[tok_kv, tok_kv_add])
                    else:
                        tok_a = v_op(lambda v, bank=bank: v.tensor_tensor(
                            KV[:, bank * 512:(bank + 1) * 512], PP[bank][:, 0:512],
                            KV[:, bank * 512:(bank + 1) * 512], ALU.add),
                            after=[tok_kv, tok_kv_add])
                    pp_free[bank] = tok_a
                    tok_kv_add = tok_a
                sc2_war = [tok for _, tok in kv_mm_toks]
                sc3_war = [tok for _, tok in kv_mm_toks]

        if ALVL < 5:
            tok_dbg_fm0 = None
            if debug:
                tok_dbg_fm0 = q_dma(lambda q: q.dma_start(out=x2_dbg, in_=FM[:, :, :]),
                                    "sQc", after=[tok_x2fm[NT - 1]])

            def finish0(final_toks):
                S.emit("sync", lambda q: q.nop(), None, after=final_toks)
                block = stack.enter_context(nc.Block())

                def mk_emitter(eng_ops):
                    def emitter(eng):
                        guaranteed = {}
                        for waits, fn, inc in eng_ops:
                            bysem = {}
                            for s_, v_ in waits:
                                bysem[s_] = max(bysem.get(s_, 0), v_)
                            for s_, v_ in bysem.items():
                                if guaranteed.get(s_, 0) < v_:
                                    eng.wait_ge(sems[s_], v_)
                                    guaranteed[s_] = v_
                            inst = fn(eng)
                            if inc is not None:
                                s_, amt = inc
                                inst.then_inc(sems[s_], amt)
                    return emitter

                for name_ in ENGINES:
                    getattr(block, name_)(mk_emitter(S.ops[name_]))

            ft = [tok_dbg_fm0, tok_x2fm[NT - 1]]
            if ALVL >= 1:
                ft.append(tok_kfm_last)
            if ALVL >= 2:
                ft.extend(kp_toks)
            if ALVL >= 3:
                ft.extend(v_ev)
            if ALVL >= 4:
                ft.append(tok_kv_add)
            finish0(ft)
            return nc

        tok_cc1 = g_dma(lambda g: g.dma_start(out=ccin, in_=KV[:, :]), "sCCD",
                        after=[tok_kv_add])
        tok_cc2 = S.emit("gpsimd", lambda g: g.collective_compute(
            "AllReduce", ALU.add, replica_groups=[[0, 1, 2, 3], [4, 5, 6, 7]],
            ins=[ccin], outs=[ccout]), ("sCC", 1), after=[tok_cc1])
        tok_cc3 = g_dma(lambda g: g.dma_start(out=KV[:, :], in_=ccout), "sCCD",
                        after=[tok_cc2])
        tok_kvb = v_op(lambda v: v.tensor_scalar_mul(KVB[:, :], KV[:, :], 1.0 / 128.0),
                       after=[tok_cc3])
        tok_dbg_fm = None
        tok_dbg_kv = None
        if debug:
            tok_dbg_kv = q_dma(lambda q: q.dma_start(out=kv_dbg, in_=KV[:, :]), "sQc", after=[tok_cc3])
            tok_dbg_fm = q_dma(lambda q: q.dma_start(out=x2_dbg, in_=FM[:, :, :]), "sQc",
                  after=[tok_x2fm[NT - 1]])

        def finish(final_toks):
            S.emit("sync", lambda q: q.nop(), None, after=final_toks)
            block = stack.enter_context(nc.Block())

            def mk_emitter(eng_ops):
                def emitter(eng):
                    guaranteed = {}
                    for waits, fn, inc in eng_ops:
                        bysem = {}
                        for s_, v_ in waits:
                            bysem[s_] = max(bysem.get(s_, 0), v_)
                        for s_, v_ in bysem.items():
                            if guaranteed.get(s_, 0) < v_:
                                eng.wait_ge(sems[s_], v_)
                                guaranteed[s_] = v_
                        inst = fn(eng)
                        if inc is not None:
                            s_, amt = inc
                            inst.then_inc(sems[s_], amt)
                return emitter

            for name_ in ENGINES:
                getattr(block, name_)(mk_emitter(S.ops[name_]))

        if stop == 'A':
            finish([tok_kvb, tok_dbg_kv, tok_dbg_fm])
            return nc

        tok_aout = [None] * NT
        tok_xload2 = [None] * NT
        tok_xload2[0] = q_dma(lambda q: q.dma_start(out=SA[:, 0, :], in_=x_in[0]), "sQa0",
                              after=[tok_x2n_done[NT - 1], tok_x2n_done[NT - 2]])
        tok_xload2[1] = q_dma(lambda q: q.dma_start(out=SA[:, 1, :], in_=x_in[1]), "sQa1",
                              after=[tok_x2n_done[NT - 1], tok_x2n_done[NT - 2]])
        tok_qfm_last = None
        tok_b_lastp = None
        tok_b_lastq = None

        for g in range(4):
            tiles = [4 * g + j for j in range(4)]
            qfm_ev = []
            for co in range(8):
                bank = co % 2
                def qmm(tensor, co=co, bank=bank, g=g):
                    last = None
                    for kc in range(8):
                        last = tensor.matmul(
                            PP[bank][:, 0:512],
                            WB[:, WQ0 + kc * 1024 + co * 128:WQ0 + kc * 1024 + co * 128 + 128],
                            FM[:, kc, g * 512:(g + 1) * 512],
                            start=(kc == 0), stop=(kc == 7))
                    return last
                tok_q = t_op(qmm, after=[tok_x2fm[t] for t in tiles] +
                             [tok_wB, pp_free[bank]])
                pp_wr[bank] = tok_q
                tok_b_lastq = tok_q
                tok_e = s_op(lambda s, co=co, bank=bank: s.activation(
                    SC1[:, co * 512:(co + 1) * 512], PP[bank][:, 0:512], AF.Identity,
                    bias=BQF[:, co:co + 1]), after=[tok_q] + sc1_war)
                pp_free[bank] = tok_e
                qfm_ev.append(tok_e)
            sc1_war = []
            tok_qfm_last = qfm_ev[-1]

            for j, t in enumerate(tiles):
                qp_toks = []
                for pk in range(4):
                    bank = 2 + pk % 2
                    def pmm(tensor, pk=pk, bank=bank, j=j):
                        last = None
                        for hh in range(4):
                            h = 4 * pk + hh
                            last = tensor.matmul(
                                PP[bank][:, hh * 128:(hh + 1) * 128],
                                PROJD[:, (h % 2) * 128:(h % 2) * 128 + 128],
                                SC1[:, (h // 2) * 512 + j * 128:(h // 2) * 512 + j * 128 + 128],
                                start=True, stop=True)
                        return last
                    tok_p = t_op(pmm, after=[tok_qfm_last, tok_proj, pp_free[bank]])
                    pp_wr[bank] = tok_p
                    tok_b_lastp = tok_p
                    tok_pfree, outs = fourier_pack(PP[bank], 4 * pk, [tok_p])
                    pp_free[bank] = tok_pfree
                    qp_toks.extend(outs)
                sc2_war = []

                at_ev = []
                at_mms = []
                for grp2 in range(2):
                    bank = 4 + grp2
                    def amm(tensor, grp2=grp2, bank=bank):
                        last = None
                        for cc in range(4):
                            c = grp2 * 4 + cc
                            for sub in range(2):
                                h = 2 * c + sub
                                po = 64 * sub
                                for half in range(2):
                                    idx = h * 2 + half
                                    last = tensor.matmul(
                                        PP[bank][po:po + 64, cc * 128:(cc + 1) * 128],
                                        KVB[:, idx * 64:(idx + 1) * 64],
                                        SC2[:, idx * 128:(idx + 1) * 128],
                                        start=(half == 0), stop=(half == 1),
                                        tile_position=(0, po))
                        return last
                    tok_at = t_op(amm, after=qp_toks + [tok_kvb, pp_free[bank]])
                    pp_wr[bank] = tok_at
                    at_mms.append(tok_at)
                    tok_ae = v_op(lambda v, grp2=grp2, bank=bank: v.tensor_copy(
                        SC3[:, grp2 * 512:(grp2 + 1) * 512], PP[bank][:, 0:512]),
                        after=[tok_at] + sc3_war)
                    pp_free[bank] = tok_ae
                    at_ev.append(tok_ae)
                sc2_war = at_mms
                sc3_war = []

                so_slot = t % 2
                wo_readers = []
                for half in range(2):
                    bank = half
                    def womm(tensor, half=half, bank=bank):
                        for c in range(8):
                            tensor.matmul(
                                PP[bank][:, 0:512],
                                SC3[:, c * 128:(c + 1) * 128],
                                WB[:, WO0 + c * 1024 + half * 512:
                                   WO0 + c * 1024 + half * 512 + 512],
                                start=(c == 0), stop=False)
                        return tensor.matmul(
                            PP[bank][:, 0:512], SEL[0:8, 128:256],
                            BROW[0:8, half * 512:half * 512 + 512],
                            start=False, stop=True)
                    tok_wo = t_op(womm, after=at_ev + [tok_wB, pp_free[bank]])
                    pp_wr[bank] = tok_wo
                    wo_readers.append(tok_wo)
                    tok_res = v_op(lambda v, half=half, bank=bank, so_slot=so_slot, t=t:
                                   v.tensor_tensor(
                                       SO[:, so_slot, half * 512:half * 512 + 512],
                                       PP[bank][:, 0:512],
                                       SA[:, t % 2, half * 512:half * 512 + 512], ALU.add),
                                   after=[tok_wo, tok_xload2[t], sout_free[so_slot]])
                    pp_free[bank] = tok_res
                    if half == 1:
                        tok_aout[t] = q_dma(
                            lambda q, t=t, so_slot=so_slot: q.dma_start(
                                out=a_hbm[t], in_=SO[:, so_slot, :]),
                            "sQo0" if so_slot == 0 else "sQo1",
                            after=[tok_res, pp_free[0]])
                        sout_free[so_slot] = tok_aout[t]
                sc3_war = wo_readers
                if t + 2 < NT:
                    tok_xload2[t + 2] = q_dma(
                        lambda q, t2=t + 2: q.dma_start(out=SA[:, t2 % 2, :], in_=x_in[t2]),
                        "sQa0" if t % 2 == 0 else "sQa1",
                        after=[pp_free[0], pp_free[1]])

        tok_b_done = tok_aout[NT - 1]

        if stop == 'B':
            finish([tok_aout[NT - 1], tok_aout[NT - 2]])
            return nc

        for kc in range(3, 7):
            w1_toks[kc] = g_dma(lambda g, kc=kc: g.dma_start(
                out=WB[:, w1col(kc):w1col(kc) + 4096], in_=w1_in[:, kc, :]), "sW2",
                after=[tok_kv_add])
        w1_toks[7] = g_dma(lambda g: g.dma_start(
            out=WB[:, w1col(7):w1col(7) + 4096], in_=w1_in[:, 7, :]), "sW2",
            after=[tok_b_lastq])
        tok_w1_all = [w1_toks[kc] for kc in range(8)]
        tok_w2a = g_dma(lambda g: g.dma_start(
            out=WB[:, 20480:20480 + 12288].rearrange("p (c f) -> p c f", f=1024),
            in_=w2_in[:, 0:12, :]), "sW0", after=[tok_b_done, tok_b_lastp])

        tok_ain = [None] * NT
        tok_ain[0] = q_dma(lambda q: q.dma_start(out=SA[:, 0, :], in_=a_hbm[0]), "sQa0",
                           after=[tok_aout[0], pp_free[0], pp_free[1]])
        tok_ain[1] = q_dma(lambda q: q.dma_start(out=SA[:, 1, :], in_=a_hbm[1]), "sQa1",
                           after=[tok_aout[1], pp_free[0], pp_free[1]])
        tok_hout = [None] * NT
        hb_war = []
        sc1_war = [tok_b_lastp]

        for g2 in range(8):
            tiles = [2 * g2, 2 * g2 + 1]
            x3_toks = []
            for j, t in enumerate(tiles):
                sal = SA[:, t % 2, :]
                tok_st = ln_stats(lambda sal=sal: sal, [tok_ain[t]])
                tok_x3n = s_op(lambda s, sal=sal: s.activation(
                    X2N[:, :], sal, AF.Identity, bias=STT[:, 17:18], scale=STT[:, 16:17]),
                    after=[tok_st, pt_free])
                stt_war = [tok_x3n]
                if t + 2 < NT:
                    tok_ain[t + 2] = q_dma(
                        lambda q, t2=t + 2: q.dma_start(out=SA[:, t2 % 2, :], in_=a_hbm[t2]),
                        "sQa0" if t % 2 == 0 else "sQa1",
                        after=[tok_x3n, tok_aout[t + 2]])
                def tp(tensor):
                    last = None
                    for c in range(8):
                        last = tensor.transpose(PT[:, c * 128:(c + 1) * 128],
                                                X2N[:, c * 128:(c + 1) * 128], IDN[:, :])
                    return last
                tok_T = t_op(tp, after=[tok_x3n, pt_free])
                ev = []
                for c in range(8):
                    ev.append(v_op(lambda v, c=c, j=j: v.tensor_scalar(
                        SC1[:, c * 256 + j * 128:c * 256 + j * 128 + 128],
                        PT[:, c * 128:(c + 1) * 128],
                        LN2G[:, c:c + 1], LN2B[:, c:c + 1], ALU.mult, ALU.add),
                        after=[tok_T] + sc1_war))
                sc1_war = []
                pt_free = ev[-1]
                x3_toks.append(ev[-1])

            hc_ev = []
            w1_mms = []
            for hc in range(NHC):
                bank = hc % 4
                def w1mm(tensor, hc=hc, bank=bank):
                    last = None
                    for kc in range(8):
                        last = tensor.matmul(
                            PP[bank][:, 0:256],
                            WB[:, w1col(kc) + hc * 128:w1col(kc) + hc * 128 + 128],
                            SC1[:, 0:2048].rearrange("p (c f) -> p c f", f=256)[:, kc, :],
                            start=(kc == 0), stop=(kc == 7))
                    return last
                tok_w1 = t_op(w1mm, after=x3_toks + tok_w1_all + [pp_free[bank]])
                pp_wr[bank] = tok_w1
                w1_mms.append(tok_w1)
                def hevac(s, hc=hc, bank=bank):
                    o = HB[:, hc * 128:hc * 128 + 4096 + 128].rearrange(
                        "p (j f) -> p j f", f=128)[:, ::32, :]
                    return s.activation(
                        o, PP[bank][:, 0:256].rearrange("p (j f) -> p j f", f=128),
                        AF.Relu, bias=B1F[:, hc:hc + 1])
                tok_he = s_op(hevac, after=[tok_w1] + hb_war)
                pp_free[bank] = tok_he
                hc_ev.append(tok_he)
            hb_war = []
            sc1_war = w1_mms
            for j, t in enumerate(tiles):
                tok_hout[t] = q_dma(
                    lambda q, t=t, j=j: q.dma_start(
                        out=h_hbm[t], in_=HB[:, j * 4096:(j + 1) * 4096]),
                    "sH0" if t % 2 == 0 else "sH1", after=hc_ev)
            hb_war = [tok_hout[t] for t in tiles]

        tok_c_done = [tok_hout[NT - 1], tok_hout[NT - 2]]

        if stop == 'C':
            finish(tok_c_done)
            return nc
        tok_w2b = g_dma(lambda g: g.dma_start(
            out=WB[:, 0:16384].rearrange("p (c f) -> p c f", f=1024),
            in_=w2_in[:, 12:28, :]), "sW1", after=tok_c_done + sc1_war)
        tok_w2c = g_dma(lambda g: g.dma_start(
            out=WB[:, 16384:20480].rearrange("p (c f) -> p c f", f=1024),
            in_=w2_in[:, 28:32, :]), "sW1", after=tok_c_done + sc1_war)

        tok_hin = [None] * NT
        tok_hin[0] = q_dma(lambda q: q.dma_start(out=HB[:, 0:4096], in_=h_hbm[0]), "sH0",
                           after=tok_c_done)
        tok_hin[1] = q_dma(lambda q: q.dma_start(out=HB[:, 4096:8192], in_=h_hbm[1]), "sH1",
                           after=tok_c_done)
        tok_ain2 = [None] * NT
        tok_ain2[0] = q_dma(lambda q: q.dma_start(out=SA[:, 0, :], in_=a_hbm[0]), "sQa0",
                            after=[x3_toks[-1]])
        tok_ain2[1] = q_dma(lambda q: q.dma_start(out=SA[:, 1, :], in_=a_hbm[1]), "sQa1",
                            after=[x3_toks[-1]])
        tok_xin3 = [None] * NT
        tok_xin3[0] = q_dma(lambda q: q.dma_start(out=SBf[:, 0, :], in_=x_in[0]), "sQb0")
        tok_xin3[1] = q_dma(lambda q: q.dma_start(out=SBf[:, 1, :], in_=x_in[1]), "sQb1")

        tok_x4out = [None] * NT
        tok_x4fm = [None] * NT
        tok_scl_last = None
        tok_sclr_last = None
        fm_war = [tok_b_lastq, tok_dbg_fm]
        tok_sclz = v_op(lambda v: v.memset(SC2[:, 2048:4096], 0.0), after=sc2_war)
        sc2_war = [tok_sclz]

        for t in range(NT):
            slot = t % 2
            w2toks = []
            for half in range(2):
                bank = half
                def w2mm(tensor, half=half, bank=bank, slot=slot):
                    for hc in range(NHC):
                        tensor.matmul(
                            PP[bank][:, 0:512],
                            HB[:, slot * 4096 + hc * 128:slot * 4096 + hc * 128 + 128],
                            WB[:, w2col(hc) + half * 512:w2col(hc) + half * 512 + 512],
                            start=(hc == 0), stop=False)
                    return tensor.matmul(
                        PP[bank][:, 0:512], SEL[0:8, 256:384],
                        BROW[0:8, half * 512:half * 512 + 512],
                        start=False, stop=True)
                aft = [tok_hin[t], tok_w2a, tok_w2b, tok_w2c, pp_free[bank]]
                tok_w2 = t_op(w2mm, after=aft)
                pp_wr[bank] = tok_w2
                w2toks.append(tok_w2)
            if t + 2 < NT:
                tok_hin[t + 2] = q_dma(
                    lambda q, t2=t + 2: q.dma_start(
                        out=HB[:, (t2 % 2) * 4096:(t2 % 2) * 4096 + 4096], in_=h_hbm[t2]),
                    "sH0" if t % 2 == 0 else "sH1", after=w2toks)
            pr = []
            for half in range(2):
                t1 = v_op(lambda v, half=half, slot=slot: v.tensor_tensor(
                    FS1[:, half * 512:half * 512 + 512], PP[half][:, 0:512],
                    SA[:, slot, half * 512:half * 512 + 512], ALU.add),
                    after=[w2toks[half], tok_ain2[t]] + fs_war)
                pp_free[half] = t1
                t2 = v_op(lambda v, half=half, slot=slot: v.tensor_tensor(
                    FS1[:, half * 512:half * 512 + 512], FS1[:, half * 512:half * 512 + 512],
                    SBf[:, slot, half * 512:half * 512 + 512], ALU.add),
                    after=[t1, tok_xin3[t]])
                pr.append(t2)
            fs_war = []
            if t + 2 < NT:
                tok_ain2[t + 2] = q_dma(
                    lambda q, t2=t + 2: q.dma_start(out=SA[:, t2 % 2, :], in_=a_hbm[t2]),
                    "sQa0" if t % 2 == 0 else "sQa1", after=pr)
                tok_xin3[t + 2] = q_dma(
                    lambda q, t2=t + 2: q.dma_start(out=SBf[:, t2 % 2, :], in_=x_in[t2]),
                    "sQb0" if t % 2 == 0 else "sQb1", after=pr)
            tok_st = ln_stats(lambda: FS1[:, :], pr)
            tok_x4 = s_op(lambda s, slot=slot: s.activation(
                SO[:, slot, :], FS1[:, :], AF.Identity, bias=STT[:, 17:18],
                scale=STT[:, 16:17]), after=[tok_st, sout_free[slot]])
            tok_x4b = s_op(lambda s: s.activation(
                X2N[:, :], FS1[:, :], AF.Identity, bias=STT[:, 17:18],
                scale=STT[:, 16:17]), after=[tok_st, pt_free])
            stt_war = [tok_x4, tok_x4b]
            fs_war = [tok_x4, tok_x4b]
            tok_x4out[t] = q_dma(
                lambda q, t=t, slot=slot: q.dma_start(out=x4_hbm[t], in_=SO[:, slot, :]),
                "sQo0" if slot == 0 else "sQo1", after=[tok_x4])
            sout_free[slot] = tok_x4out[t]

            def tp(tensor):
                last = None
                for c in range(8):
                    last = tensor.transpose(PT[:, c * 128:(c + 1) * 128],
                                            X2N[:, c * 128:(c + 1) * 128], IDN[:, :])
                return last
            tok_T = t_op(tp, after=[tok_x4b, pt_free])
            ev = []
            for c in range(8):
                ev.append(v_op(lambda v, c=c, t=t: v.tensor_copy(
                    FM[:, c, t * 128:(t + 1) * 128], PT[:, c * 128:(c + 1) * 128]),
                    after=[tok_T] + fm_war))
            fm_war = []
            pt_free = ev[-1]
            tok_x4fm[t] = ev[-1]

            def gmm(tensor, t=t):
                last = None
                for kc in range(8):
                    last = tensor.matmul(
                        PP[2][:, 0:8], FM[:, kc, t * 128:(t + 1) * 128],
                        GWB[:, kc * 8:(kc + 1) * 8],
                        start=(kc == 0), stop=(kc == 7))
                return last
            tok_gate = t_op(gmm, after=ev + [pp_free[2], tok_gc])
            pp_wr[2] = tok_gate
            t1 = v_op(lambda v: v.reduce_max(STT[:, 18:19], PP[2][:, 0:8], axis=AX.X,
                                             negate=True), after=[tok_gate] + stt_war)
            t1b = v_op(lambda v: v.reduce_max(STT[:, 19:20], PP[2][:, 0:8], axis=AX.X),
                       after=[tok_gate])
            t2 = s_op(lambda s: s.activation(STT[:, 20:28], PP[2][:, 0:8], AF.Exp,
                                             bias=STT[:, 18:19]), after=[t1])
            t3 = v_op(lambda v: v.reduce_sum(STT[:, 28:29], STT[:, 20:28], axis=AX.X),
                      after=[t2])
            t4 = v_op(lambda v: v.reciprocal(STT[:, 29:30], STT[:, 28:29]), after=[t3])
            t5 = v_op(lambda v: v.tensor_scalar(STT[:, 30:38], PP[2][:, 0:8],
                                                STT[:, 19:20], None, ALU.is_ge),
                      after=[t1b, t2])
            t6 = v_op(lambda v: v.scalar_tensor_tensor(STT[:, 38:46], STT[:, 30:38],
                                                       -1e30, PP[2][:, 0:8],
                                                       ALU.mult, ALU.add), after=[t5])
            t7 = v_op(lambda v: v.reduce_max(STT[:, 46:47], STT[:, 38:46], axis=AX.X),
                      after=[t6])
            t8 = v_op(lambda v: v.tensor_scalar(STT[:, 47:55], PP[2][:, 0:8],
                                                STT[:, 46:47], None, ALU.is_ge),
                      after=[t7])
            pp_free[2] = t8
            t9 = v_op(lambda v: v.tensor_tensor(STT[:, 55:63], STT[:, 20:28],
                                                STT[:, 47:55], ALU.mult), after=[t8, t4])
            t10 = v_op(lambda v, t=t: v.tensor_scalar(SCL[:, t * 8:(t + 1) * 8],
                                                      STT[:, 55:63], STT[:, 29:30], None,
                                                      ALU.mult), after=[t9])
            tok_scl_last = t10
            t11 = s_op(lambda s: s.activation(SCLB[:, :], STT[:, 55:63], AF.Identity,
                                              scale=STT[:, 29:30]),
                       after=[t9, pt_free])
            tok_Ts = t_op(lambda tensor: tensor.transpose(PT[0:8, 0:128], SCLB[:, :],
                                                          IDN[:, :]),
                          after=[t11, pt_free] + sc2_war)
            sc2_war = []
            tok_se = v_op(lambda v, t=t: v.tensor_copy(
                SC2[0:8, 2048 + t * 128:2048 + (t + 1) * 128], PT[0:8, 0:128]),
                after=[tok_Ts])
            pt_free = tok_se
            tok_sclr_last = tok_se
            stt_war = stt_war + [t10, t11]

        tok_d_done = [tok_x4out[NT - 1], tok_sclr_last, tok_x4fm[NT - 1]]
        if debug:
            q_dma(lambda q: q.dma_start(out=scl_dbg, in_=SCL[:, :]), "sQc",
                  after=[tok_scl_last])

        if stop == 'D':
            finish(tok_d_done + [tok_x4out[NT - 2]] +
                   ([S.counts and None] if not debug else []))
            return nc

        tok_ew = {}

        def load_expert(e, sem, after):
            toks = [g_dma(lambda g, e=e: g.dma_start(
                out=WB[:, ecol(e, 0, 0):ecol(e, 0, 0) + 4096].rearrange(
                    "p (c f) -> p c f", f=512), in_=mw1_in[e]), sem, after=after),
                g_dma(lambda g, e=e: g.dma_start(
                    out=WB[:, ecol(e, 1, 0):ecol(e, 1, 0) + 4096].rearrange(
                        "p (c f) -> p c f", f=512), in_=mw3_in[e]), sem, after=after),
                g_dma(lambda g, e=e: g.dma_start(
                    out=WB[:, ecol(e, 2, 0):ecol(e, 2, 0) + 4096].rearrange(
                        "p (c f) -> p c f", f=1024), in_=mw2_in[e]), sem, after=after)]
            return toks

        tok_ew[0] = load_expert(0, "sW2", tok_c_done + sc1_war + [w1_toks[7]])

        tok_y_acc = [None] * NT
        sc1_war_e = [tok_b_lastp]
        last_euse = {}

        for r in range(4):
            e0, e1 = 2 * r, 2 * r + 1
            if r == 0:
                tok_ew[1] = load_expert(1, "sW0", tok_d_done + [pp_wr[0], pp_wr[1], tok_w2a])
                tok_ew[2] = load_expert(2, "sW1", tok_d_done + [pp_wr[0], pp_wr[1], tok_w2b, tok_w2c])
            for g in range(4):
                tiles = [4 * g + j for j in range(4)]
                srep_toks = []
                for pe, e in enumerate((e0, e1)):
                    tok_sm = t_op(lambda tensor, e=e, g=g: tensor.matmul(
                        PP[4][:, 0:512], SEL[:, e * 128:(e + 1) * 128],
                        SC2[:, 2048 + g * 512:2048 + (g + 1) * 512],
                        start=True, stop=True),
                        after=[tok_sclr_last, pp_free[4]])
                    pp_wr[4] = tok_sm
                    tok_sv = v_op(lambda v, pe=pe: v.tensor_copy(
                        FS1[:, pe * 512:(pe + 1) * 512], PP[4][:, 0:512]),
                        after=[tok_sm] + fs_war)
                    pp_free[4] = tok_sv
                    srep_toks.append(tok_sv)
                fs_war = []
                h_done = {}
                for pe, e in enumerate((e0, e1)):
                    eh = []
                    for hc in range(4):
                        ba, bc = (0, 1) if hc % 2 == 0 else (2, 3)
                        def acmm(tensor, e, hc, g, bank, piece):
                            last = None
                            for kc in range(8):
                                last = tensor.matmul(
                                    PP[bank][:, 0:512],
                                    WB[:, ecol(e, piece, kc * 512 + hc * 128):
                                       ecol(e, piece, kc * 512 + hc * 128) + 128],
                                    FM[:, kc, g * 512:(g + 1) * 512],
                                    start=(kc == 0), stop=(kc == 7))
                            return last
                        aft_w = tok_ew[e] + [tok_x4fm[t] for t in tiles]
                        tok_a = t_op(lambda tensor, e=e, hc=hc, g=g, ba=ba: acmm(
                            tensor, e, hc, g, ba, 0), after=aft_w + [pp_free[ba]])
                        pp_wr[ba] = tok_a
                        tok_c = t_op(lambda tensor, e=e, hc=hc, g=g, bc=bc: acmm(
                            tensor, e, hc, g, bc, 1), after=aft_w + [pp_free[bc]])
                        pp_wr[bc] = tok_c
                        last_euse[e] = tok_c
                        m1 = v_op(lambda v, pe=pe, ba=ba: v.tensor_tensor(
                            FS2[:, 0:512], FS1[:, pe * 512:(pe + 1) * 512],
                            PP[ba][:, 0:512], ALU.mult),
                            after=[tok_a, srep_toks[pe]] + fs_war)
                        fs_war = []
                        pp_free[ba] = m1
                        m2 = s_op(lambda s: s.activation(FS2[:, 0:512], FS2[:, 0:512],
                                                         AF.Silu), after=[m1])
                        m3 = v_op(lambda v, pe=pe, bc=bc: v.tensor_tensor(
                            FS2[:, 512:1024], FS1[:, pe * 512:(pe + 1) * 512],
                            PP[bc][:, 0:512], ALU.mult),
                            after=[tok_c, srep_toks[pe], m1])
                        pp_free[bc] = m3
                        m4 = v_op(lambda v, pe=pe, hc=hc: v.tensor_tensor(
                            SC1[:, pe * 2048 + hc * 512:pe * 2048 + (hc + 1) * 512],
                            FS2[:, 0:512], FS2[:, 512:1024], ALU.mult),
                            after=[m2, m3] + sc1_war_e)
                        sc1_war_e = []
                        eh.append(m4)
                        fs_war = [m2, m4]
                    h_done[e] = eh
                y_readers = []
                for j, t in enumerate(tiles):
                    def ymm(tensor, e0=e0, e1=e1, j=j):
                        last = None
                        for pe, e in enumerate((e0, e1)):
                            for hc in range(4):
                                for half in range(2):
                                    last = tensor.matmul(
                                        PP[4 + half][:, 0:512],
                                        SC1[:, pe * 2048 + hc * 512 + j * 128:
                                            pe * 2048 + hc * 512 + j * 128 + 128],
                                        WB[:, ecol(e, 2, hc * 1024 + half * 512):
                                           ecol(e, 2, hc * 1024 + half * 512) + 512],
                                        start=(pe == 0 and hc == 0),
                                        stop=(pe == 1 and hc == 3))
                        return last
                    tok_y = t_op(ymm, after=h_done[e0] + h_done[e1] + tok_ew[e0] +
                                 tok_ew[e1] + [pp_free[4], pp_free[5]])
                    pp_wr[4] = tok_y
                    pp_wr[5] = tok_y
                    last_euse[e0] = tok_y
                    last_euse[e1] = tok_y
                    y_readers.append(tok_y)
                    slot = t % 2
                    ye = []
                    for half in range(2):
                        ye.append(s_op(lambda s, half=half, slot=slot: s.copy(
                            SO[:, slot, half * 512:half * 512 + 512],
                            PP[4 + half][:, 0:512]),
                            after=[tok_y, sout_free[slot]]))
                    pp_free[4] = ye[0]
                    pp_free[5] = ye[1]
                    ysem = "sY0" if slot == 0 else "sY1"
                    if r == 0:
                        tok_ya = g_dma(lambda gd, t=t, slot=slot: gd.dma_start(
                            out=y_hbm[t], in_=SO[:, slot, :]), ysem, after=ye)
                    else:
                        tok_ya = g_dma(lambda gd, t=t, slot=slot: gd.dma_start(
                            out=y_hbm[t], in_=SO[:, slot, :], accum_op=ALU.add), ysem,
                            after=ye + [tok_y_acc[t]])
                    tok_y_acc[t] = tok_ya
                    sout_free[slot] = tok_ya
                sc1_war_e = y_readers
            if r == 0:
                tok_ew[3] = load_expert(3, "sW2", [last_euse[0]])
                tok_ew[4] = load_expert(4, "sW0", [last_euse[1]])
            elif r == 1:
                tok_ew[5] = load_expert(5, "sW1", [last_euse[2]])
                tok_ew[6] = load_expert(6, "sW2", [last_euse[3]])
            elif r == 2:
                tok_ew[7] = load_expert(7, "sW0", [last_euse[4]])

        if stop == 'E':
            finish([tok_y_acc[NT - 1], tok_y_acc[NT - 2]])
            return nc

        tok_yin = [None] * NT
        tok_x4in = [None] * NT
        tok_yin[0] = q_dma(lambda q: q.dma_start(out=FS1[:, :], in_=y_hbm[0]), "sH0",
                           after=[tok_y_acc[0]] + fs_war + srep_toks +
                           [m1, m3])
        tok_x4in[0] = q_dma(lambda q: q.dma_start(out=SBf[:, 0, :], in_=x4_hbm[0]), "sQb0",
                            after=[tok_x4out[0], pr[0], pr[1]])
        tok_x4in[1] = q_dma(lambda q: q.dma_start(out=SBf[:, 1, :], in_=x4_hbm[1]), "sQb1",
                            after=[tok_x4out[1], pr[0], pr[1]])
        tok_out = [None] * NT
        tok_pre = None
        tok_o = None
        for t in range(NT):
            slot = t % 2
            t1 = v_op(lambda v, slot=slot: v.tensor_tensor(
                FS2[:, :], FS1[:, :], SBf[:, slot, :], ALU.add),
                after=[tok_yin[t], tok_x4in[t], tok_pre, tok_o] + (fs_war if t == 0 else []))
            fs_war = []
            tok_pre = t1
            if t + 1 < NT:
                tok_yin[t + 1] = q_dma(
                    lambda q, t2=t + 1: q.dma_start(out=FS1[:, :], in_=y_hbm[t2]),
                    "sH0" if (t + 1) % 2 == 0 else "sH1",
                    after=[t1, tok_y_acc[t + 1]])
            if t + 2 < NT:
                tok_x4in[t + 2] = q_dma(
                    lambda q, t2=t + 2: q.dma_start(out=SBf[:, t2 % 2, :], in_=x4_hbm[t2]),
                    "sQb0" if t % 2 == 0 else "sQb1", after=[t1])
            tok_st = ln_stats(lambda: FS2[:, :], [t1])
            tok_o = s_op(lambda s, slot=slot: s.activation(
                SO[:, slot, :], FS2[:, :], AF.Identity, bias=STT[:, 17:18],
                scale=STT[:, 16:17]), after=[tok_st, sout_free[slot]])
            stt_war = [tok_o]
            tok_out[t] = q_dma(lambda q, t=t, slot=slot: q.dma_start(
                out=out_hbm[t], in_=SO[:, slot, :]),
                "sQo0" if slot == 0 else "sQo1", after=[tok_o])
            sout_free[slot] = tok_out[t]

        finish([tok_out[NT - 1], tok_out[NT - 2]])

    return nc


def _layout_inputs(L0_attn_x, L0_attn_proj_mat, L0_attn_wq, L0_attn_bq, L0_attn_wk,
                   L0_attn_bk, L0_attn_wv, L0_attn_bv, L0_attn_wo, L0_attn_bo,
                   L0_attn_ln1_g, L0_attn_ln1_b, L0_attn_w1, L0_attn_b1, L0_attn_w2,
                   L0_attn_b2, L0_attn_ln2_g, L0_attn_ln2_b, L0_ffn_gate_w,
                   L0_ffn_w1, L0_ffn_w2, L0_ffn_w3):
    f = np.float32
    bf = ml_dtypes.bfloat16

    def wlayout(w, nk):
        K, F = np.asarray(w).shape
        return np.ascontiguousarray(np.asarray(w, f).reshape(nk, 128, F).transpose(1, 0, 2))

    def vfm(v, nc_):
        return np.ascontiguousarray(np.asarray(v, f).reshape(nc_, 128).T)

    common = dict(
        wq_in=wlayout(L0_attn_wq, 8), wk_in=wlayout(L0_attn_wk, 8),
        wv_in=wlayout(L0_attn_wv, 8), wo_in=wlayout(L0_attn_wo, 8),
        w1_in=wlayout(L0_attn_w1, 8), w2_in=wlayout(L0_attn_w2, 32),
        mw1_in=np.ascontiguousarray(
            np.asarray(L0_ffn_w1, f).reshape(8, 8, 128, 512).transpose(0, 2, 1, 3)),
        mw3_in=np.ascontiguousarray(
            np.asarray(L0_ffn_w3, f).reshape(8, 8, 128, 512).transpose(0, 2, 1, 3)),
        mw2_in=np.ascontiguousarray(
            np.asarray(L0_ffn_w2, f).reshape(8, 4, 128, 1024).transpose(0, 2, 1, 3)),
        proj_in=np.ascontiguousarray(np.asarray(L0_attn_proj_mat, f)),
        ln1g_in=vfm(L0_attn_ln1_g, 8), ln1b_in=vfm(L0_attn_ln1_b, 8),
        ln2g_in=vfm(L0_attn_ln2_g, 8), ln2b_in=vfm(L0_attn_ln2_b, 8),
        bqf_in=vfm(L0_attn_bq, 8), bkf_in=vfm(L0_attn_bk, 8),
        b1f_in=vfm(L0_attn_b1, 32),
        bv_in=np.ascontiguousarray(np.asarray(L0_attn_bv, f).reshape(1, 1024)),
        bo_in=np.ascontiguousarray(np.asarray(L0_attn_bo, f).reshape(1, 1024)),
        b2_in=np.ascontiguousarray(np.asarray(L0_attn_b2, f).reshape(1, 1024)),
        gate_in=np.ascontiguousarray(
            np.asarray(L0_ffn_gate_w, f).reshape(8, 128, 8).transpose(1, 0, 2).reshape(128, 64)),
        ident_in=np.eye(128, dtype=bf),
        cst_in=np.ascontiguousarray(np.broadcast_to(
            np.array([PI / 2, 0, 0, 0, 0, 0, 0, 0], f), (128, 8))),
    )
    sel = np.zeros((128, 1024), dtype=bf)
    for e in range(8):
        sel[e, e * 128:(e + 1) * 128] = 1
    common["sel_in"] = sel

    x = np.asarray(L0_attn_x, f).reshape(16384, 1024)
    in_maps = []
    for c in range(8):
        m = dict(common)
        m["x_in"] = np.ascontiguousarray(
            x[c * 2048:(c + 1) * 2048].reshape(16, 128, 1024))
        in_maps.append(m)
    return in_maps


def kernel(**inputs):
    in_maps = _layout_inputs(**inputs)
    nc = build()
    res = run_bass_kernel_spmd(nc, in_maps, core_ids=list(range(8)))
    outs = [np.asarray(r["out_hbm"], np.float32).reshape(2048, 1024)
            for r in res.results]
    return np.concatenate(outs, axis=0).reshape(2, 8192, 1024)
